# version 03 (final)
# baseline (speedup 1.0000x reference)
import numpy as np
import ml_dtypes
import concourse.bass as bass
import concourse.mybir as mybir
from concourse.bass_utils import run_bass_kernel_spmd
from contextlib import ExitStack

F32 = mybir.dt.float32
BF16 = mybir.dt.bfloat16
U8 = mybir.dt.uint8
AF = mybir.ActivationFunctionType
ALU = mybir.AluOpType
AX = mybir.AxisListType.X

NCORE = 8
D = 4096
S = 2048
B = 4
NH = 32
HD = 128
I = 11008
TOK = B * S
TPC = TOK // NCORE
NTT = TPC // 128
HPC = NH // NCORE
KC = D // 128
ICN = I // 128
GG = 32
GD = 86
EPS = 1e-5
CLIP = 8.0
RS128 = float(HD) ** -0.5

NBLK_CH = [22, 22, 21, 21]
NBLK_OFF = [0, 22, 44, 65]

_STAGE = 3
_SUB = 0
_FFN_PARTS = frozenset(("zs", "silu", "hg"))


def _alibi_slopes():
    m = np.arange(1, NH + 1, dtype=np.float64) * (8.0 / NH)
    return (1.0 / np.power(2.0, m)).astype(np.float32)


class Sem:
    def __init__(self, ctx, nc, name):
        self.h = ctx.enter_context(nc.semaphore(name))
        self.n = 0

    def inc(self, k=1):
        self.n += k
        return self.n

    def dma(self):
        return self.inc(16)


class Prog:
    ENGINES = ("sync", "gpsimd", "tensor", "vector", "scalar")

    def __init__(self):
        self.ops = {e: [] for e in self.ENGINES}

    def emit(self, engine, fn):
        self.ops[engine].append(fn)

    def run(self, engine, eng):
        for fn in self.ops[engine]:
            fn(eng)


def build_program():
    nc = bass.Bass("TRN2", target_bir_lowering=False, debug=False)
    ctx = ExitStack()

    def din(name, shape, dt):
        return nc.dram_tensor(name, shape, dt, kind="ExternalInput")

    x_in = din("x_in", [TPC, D], F32)
    wqkvT = din("wqkvT", [D, 3 * 512], F32)
    woutT = din("woutT", [D, D], F32)
    qln_w = din("qln_w", [1, 512], F32)
    qln_b = din("qln_b", [1, 512], F32)
    kln_w = din("kln_w", [1, 512], F32)
    kln_b = din("kln_b", [1, 512], F32)
    slopes_in = din("slopes_in", [1, HPC], F32)
    g_bytes = din("g_bytes", [512, 5504], U8)
    u_bytes = din("u_bytes", [512, 5504], U8)
    d_bytes = din("d_bytes", [I, 256], U8)
    gs_full = din("gs_full", [GG, I], F32)
    us_full = din("us_full", [GG, I], F32)
    ds_full = din("ds_full", [GD, D], F32)
    gz_bytes = din("gz_bytes", [GG, 5504], U8)
    uz_bytes = din("uz_bytes", [GG, 5504], U8)
    dz_bytes = din("dz_bytes", [GD, 2048], U8)
    gs_mine = din("gs_mine", [4, I], F32)
    us_mine = din("us_mine", [4, I], F32)
    ds_mine = din("ds_mine", [GD, 512], F32)
    ident_in = din("ident_in", [128, 128], BF16)
    stair32_in = din("stair32_in", [128, 64], BF16)
    stair86_in = din("stair86_in", [128, 172], BF16)
    maskdiag_in = din("maskdiag_in", [128, 128], F32)
    rmast_in = din("rmast_in", [128, 2560], F32)

    y_out = nc.dram_tensor("y_out", [TPC, D], F32, kind="ExternalOutput")

    wg_in = [nc.dram_tensor(f"wg_in{b}", [4, NBLK_CH[b], 128, 128], BF16)
             for b in range(4)]
    wu_in = [nc.dram_tensor(f"wu_in{b}", [4, NBLK_CH[b], 128, 128], BF16)
             for b in range(4)]
    wg_c = [nc.dram_tensor(f"wg_c{b}", [NCORE, 4, NBLK_CH[b], 128, 128], BF16,
                           ) for b in range(4)]
    wu_c = [nc.dram_tensor(f"wu_c{b}", [NCORE, 4, NBLK_CH[b], 128, 128], BF16,
                           ) for b in range(4)]
    wd_in = nc.dram_tensor("wd_in", [ICN, 128, 512], BF16)
    wd_c = nc.dram_tensor("wd_c", [NCORE, ICN, 128, 512], BF16)
    zsg_d = nc.dram_tensor("zsg_d", [GG, I], BF16)
    zsu_d = nc.dram_tensor("zsu_d", [GG, I], BF16)
    zsd_d = nc.dram_tensor("zsd_d", [GD, D], BF16)
    sg_bf = nc.dram_tensor("sg_bf", [4, I], BF16)
    su_bf = nc.dram_tensor("su_bf", [4, I], BF16)
    sd_bf = nc.dram_tensor("sd_bf", [GD, 512], BF16)
    sgf_bf = nc.dram_tensor("sgf_bf", [GG, I], BF16)
    suf_bf = nc.dram_tensor("suf_bf", [GG, I], BF16)
    sdf_bf = nc.dram_tensor("sdf_bf", [GD, D], BF16)
    x_parkA = nc.dram_tensor("x_parkA", [TPC, D], F32)
    x_parkB = nc.dram_tensor("x_parkB", [TPC, D], F32)
    h_cache = nc.dram_tensor("h_cache", [ICN, 128, TPC], BF16)
    agx_in = nc.dram_tensor("agx_in", [D, TPC], BF16)
    agx_out = nc.dram_tensor("agx_out", [NCORE, D, TPC], BF16)
    qkv_park = nc.dram_tensor("qkv_park", [TOK, 3 * 512], BF16)
    st_in = nc.dram_tensor("st_in", [TOK // 128, 128, 4], F32)
    st_out = nc.dram_tensor("st_out", [TOK // 128, 128, 4], F32)
    qhat_d = nc.dram_tensor("qhat_d", [512, TOK], BF16)
    khat_d = nc.dram_tensor("khat_d", [512, TOK], BF16)
    o_in = nc.dram_tensor("o_in", [NCORE, 512, TPC], BF16)
    o_out = nc.dram_tensor("o_out", [NCORE, 512, TPC], BF16)
    wqkv_bf = nc.dram_tensor("wqkv_bf", [D, 3 * 512], BF16)
    wout_bf = nc.dram_tensor("wout_bf", [D, D], BF16)

    e = ctx.enter_context
    big = e(nc.sbuf_tensor("big", [128, KC * TPC], BF16))
    wbuf = e(nc.sbuf_tensor("wbuf", [128, 32768], BF16))
    wbuf_f32 = wbuf.bitcast(F32)
    u8a = e(nc.sbuf_tensor("u8a", [128, 16640], U8))
    bf16a = e(nc.sbuf_tensor("bf16a", [128, 2048], BF16))
    smalls = e(nc.sbuf_tensor("smalls", [128, 64], F32))
    stats_sb = e(nc.sbuf_tensor("stats_sb", [128, 256], F32))
    consts = e(nc.sbuf_tensor("consts", [128, 2692], F32))
    cbf = e(nc.sbuf_tensor("cbf", [128, 384], BF16))
    zs32 = e(nc.sbuf_tensor("zs32", [GG, 5632], BF16))
    zs86 = e(nc.sbuf_tensor("zs86", [GD, 1024], BF16))
    sums = e(nc.sbuf_tensor("sums", [128, 2048], BF16))

    ps = [e(nc.psum_tensor(f"ps{i}", [128, 512], F32)) for i in range(8)]
    ps_bf = [t.bitcast(BF16) for t in ps]

    sems = {}
    for nm in ("ld", "ld2", "ldz", "gd", "cc", "pe", "ve", "sc", "ex", "ev",
               "st", "ph", "dq", "hsum", "qs", "at", "cld",
               "xld0", "xld1", "wld0", "wld1", "dld0", "dld1", "phs0", "phs1",
               "b_sync", "b_gpsimd", "b_tensor", "b_vector", "b_scalar", "qlb", "upk"):
        sems[nm] = Sem(ctx, nc, "s_" + nm)
    for i in range(8):
        sems[f"dxl{i}"] = Sem(ctx, nc, f"s_dxl{i}")
        sems[f"ost{i}"] = Sem(ctx, nc, f"s_ost{i}")

    p = Prog()

    def dma(queue, out, in_, sem, waits=()):
        v = sem.dma()
        ws = tuple(waits)

        def f(eng):
            for (s2, v2) in ws:
                eng.wait_ge(s2.h, v2)
            eng.dma_start(out=out, in_=in_).then_inc(sem.h, 16)
        p.emit(queue, f)
        return v

    def bcast_row(dram_t, row, col0, width, parts=128):
        a = dram_t[row:row + 1, col0:col0 + width]
        return bass.AP(a.tensor, a.offset, [[0, parts], [1, width]])

    def barrier():
        vals = {}
        for en in Prog.ENGINES:
            vals[en] = sems["b_" + en].inc()

        def mk(en):
            def f(eng):
                if en == "vector":
                    eng.engine_nop().then_inc(sems["b_" + en].h, 1)
                else:
                    eng.nop().then_inc(sems["b_" + en].h, 1)
                for e2 in Prog.ENGINES:
                    eng.wait_ge(sems["b_" + e2].h, vals[e2])
            return f
        for en in Prog.ENGINES:
            p.emit(en, mk(en))

    def phase_consts():
        dma("sync", consts[:, 0:2560], rmast_in.ap(), sems["cld"])
        dma("sync", consts[:, 2560:2688], maskdiag_in.ap(), sems["cld"])
        dma("sync", consts[:, 2688:2692], bcast_row(slopes_in, 0, 0, HPC), sems["cld"])
        dma("sync", cbf[:, 0:128], ident_in.ap(), sems["cld"])
        dma("sync", cbf[:, 128:192], stair32_in.ap(), sems["cld"])
        dma("sync", cbf[:, 192:364], stair86_in.ap(), sems["cld"])
        for i, t in enumerate((qln_w, qln_b, kln_w, kln_b)):
            dma("gpsimd", bf16a[:, i * 512:(i + 1) * 512],
                bcast_row(t, 0, 0, 512), sems["qlb"])
        gdv = sems["qlb"].n
        vv = sems["ve"].inc()

        def f(eng):
            eng.wait_ge(sems["qlb"].h, gdv)
            eng.tensor_scalar(bf16a[:, 512:1024], bf16a[:, 512:1024],
                              RS128, None, ALU.mult).then_inc(sems["ve"].h, 1)
        p.emit("vector", f)

        if _STAGE >= 2:
            for i in range(8):
                dma("gpsimd", wqkv_bf[512 * i:512 * (i + 1), :],
                    wqkvT[512 * i:512 * (i + 1), :], sems["gd"])
            for i in range(16):
                dma("gpsimd", wout_bf[256 * i:256 * (i + 1), :],
                    woutT[256 * i:256 * (i + 1), :], sems["gd"])
        dma("gpsimd", sg_bf.ap(), gs_mine.ap(), sems["gd"])
        dma("gpsimd", su_bf.ap(), us_mine.ap(), sems["gd"])
        dma("gpsimd", sd_bf.ap(), ds_mine.ap(), sems["gd"])
        dma("gpsimd", sgf_bf.ap(), gs_full.ap(), sems["gd"])
        dma("gpsimd", suf_bf.ap(), us_full.ap(), sems["gd"])
        dma("gpsimd", sdf_bf.ap(), ds_full.ap(), sems["gd"])

    def unpack_bytes(eng, dst_u8, src_u8, nint32, esem=None):
        u = sems["upk"]
        nib4 = dst_u8.rearrange("p (c u v) -> p c u v", u=2, v=4)
        byt4 = src_u8.rearrange("p (c b u) -> p c b u", b=4, u=1)
        ins = None
        for shift, b0, v in ((0, 0, 0), (0, 2, 1), (4, 0, 2), (4, 2, 3)):
            if ins is not None:
                u.n += 1
                ins.then_inc(u.h, 1)
                eng.wait_ge(u.h, u.n)
            ins = eng.tensor_scalar(
                nib4[:, :, :, v:v + 1], byt4[:, :, b0:b0 + 2, :],
                shift, 15, ALU.logical_shift_right, ALU.bitwise_and)
        return ins

    def phase_ln(x_src, to_ag, upto=99, entry=()):
        fap_v = [0, 0]
        tr_v = 0
        bank_ev = {4: 0, 5: 0, 6: 0, 7: 0}
        for tt in range(NTT):
            slot = tt % 2
            xoff = slot * 4096
            sb = slot * 16
            xsem = sems["xld%d" % slot]
            w0 = ([(sems["sc"], fap_v[slot])] if fap_v[slot] else []) + \
                 (list(entry) if tt < 2 else [])
            v_x = dma("sync", wbuf_f32[:, xoff:xoff + D],
                      x_src[tt * 128:(tt + 1) * 128, :], xsem, waits=w0)
            if upto < 1:
                continue
            scv = sems["sc"].inc()

            def fsc(eng, xoff=xoff, v_x=v_x, sb=sb, xsem=xsem):
                eng.wait_ge(xsem.h, v_x)
                eng.activation(wbuf_f32[:, 8192:8192 + D],
                               wbuf_f32[:, xoff:xoff + D], AF.Square,
                               accum_out=smalls[:, sb + 1:sb + 2]).then_inc(sems["sc"].h, 1)
            p.emit("scalar", fsc)
            if upto < 2:
                continue
            vv1 = sems["ve"].inc()

            def fve(eng, xoff=xoff, v_x=v_x, sb=sb, xsem=xsem):
                eng.wait_ge(xsem.h, v_x)
                eng.reduce_sum(smalls[:, sb:sb + 1], wbuf_f32[:, xoff:xoff + D],
                               axis=AX).then_inc(sems["ve"].h, 1)
            p.emit("vector", fve)
            if upto < 3:
                continue

            vvA = sems["ve"].n + 3
            sems["ve"].inc(3)
            scv2 = sems["sc"].inc()
            vvB = sems["ve"].n + 2
            sems["ve"].inc(2)

            def ffin(eng, sb=sb, vv1=vv1, scv=scv, base=vvA - 3):
                eng.wait_ge(sems["ve"].h, vv1)
                eng.wait_ge(sems["sc"].h, scv)
                eng.tensor_scalar(smalls[:, sb + 2:sb + 3], smalls[:, sb:sb + 1],
                                  1.0 / D, None, ALU.mult)
                eng.tensor_scalar(smalls[:, sb + 3:sb + 4], smalls[:, sb + 1:sb + 2],
                                  1.0 / D, None, ALU.mult).then_inc(sems["ve"].h, 1)
                eng.wait_ge(sems["ve"].h, base + 1)
                eng.scalar_tensor_tensor(
                    smalls[:, sb + 4:sb + 5], smalls[:, sb + 2:sb + 3],
                    smalls[:, sb + 2:sb + 3], smalls[:, sb + 3:sb + 4],
                    ALU.mult, ALU.subtract).then_inc(sems["ve"].h, 1)
                eng.wait_ge(sems["ve"].h, base + 2)
                eng.tensor_scalar(smalls[:, sb + 5:sb + 6], smalls[:, sb + 4:sb + 5],
                                  -1.0, EPS, ALU.mult, ALU.add).then_inc(sems["ve"].h, 1)
            p.emit("vector", ffin)

            def fsq(eng, sb=sb, vvA=vvA):
                eng.wait_ge(sems["ve"].h, vvA)
                eng.activation(smalls[:, sb + 6:sb + 7], smalls[:, sb + 5:sb + 6],
                               AF.Sqrt).then_inc(sems["sc"].h, 1)
            p.emit("scalar", fsq)

            def frec(eng, sb=sb, scv2=scv2, base=vvB - 2):
                eng.wait_ge(sems["sc"].h, scv2)
                eng.reciprocal(smalls[:, sb + 7:sb + 8],
                               smalls[:, sb + 6:sb + 7]).then_inc(sems["ve"].h, 1)
                eng.wait_ge(sems["ve"].h, base + 1)
                eng.tensor_scalar(smalls[:, sb + 4:sb + 5], smalls[:, sb + 7:sb + 8],
                                  smalls[:, sb + 2:sb + 3], -1.0,
                                  ALU.mult, ALU.mult).then_inc(sems["ve"].h, 1)
            p.emit("vector", frec)
            vv3 = vvB

            fap_v[slot] = sems["sc"].inc()
            scv3 = fap_v[slot]

            def fap(eng, xoff=xoff, sb=sb, vv3=vv3, tr_pre=tr_v):
                eng.wait_ge(sems["ve"].h, vv3)
                if tr_pre:
                    eng.wait_ge(sems["pe"].h, tr_pre)
                eng.activation(wbuf[:, 28160:28160 + D], wbuf_f32[:, xoff:xoff + D],
                               AF.Identity, bias=smalls[:, sb + 4:sb + 5],
                               scale=smalls[:, sb + 7:sb + 8]).then_inc(sems["sc"].h, 1)
            p.emit("scalar", fap)
            if upto < 5:
                continue

            for kc in range(KC):
                bank = 4 + (kc % 4)
                bfree = bank_ev[bank]
                pev = sems["pe"].inc()

                def ftr(eng, kc=kc, bank=bank, scv3=scv3, bfree=bfree,
                        tt=tt, entry=tuple(entry)):
                    eng.wait_ge(sems["cld"].h, sems["cld"].n)
                    eng.wait_ge(sems["sc"].h, scv3)
                    if tt == 0 and kc < 4:
                        for (s2, v2) in entry:
                            eng.wait_ge(s2.h, v2)
                    if bfree:
                        eng.wait_ge(sems["ev"].h, bfree)
                    eng.transpose(ps_bf[bank][:, 0:128],
                                  wbuf[:, 28160 + kc * 128:28160 + (kc + 1) * 128],
                                  cbf[:, 0:128]).then_inc(sems["pe"].h, 1)
                p.emit("tensor", ftr)
                tr_v = pev
                evv = sems["ev"].inc()
                bank_ev[bank] = evv

                def fev(eng, kc=kc, bank=bank, tt=tt, pev=pev):
                    eng.wait_ge(sems["pe"].h, pev)
                    t = big.ap()
                    dst = bass.AP(t.tensor, kc * TPC + tt * 128,
                                  [[KC * TPC, 128], [1, 128]])
                    eng.tensor_copy(dst, ps_bf[bank][:, 0:128]).then_inc(sems["ev"].h, 1)
                p.emit("vector", fev)

        if to_ag:
            evv = sems["ev"].n
            gdv = sems["gd"].dma()

            def fout(eng, evv=evv):
                eng.wait_ge(sems["ev"].h, evv)
                eng.dma_start(
                    out=bass.AP(agx_in.ap().tensor, 0,
                                [[TPC, 128], [128 * TPC, KC], [1, TPC]]),
                    in_=big[:, 0:KC * TPC].rearrange("p (k t) -> p k t", k=KC)
                ).then_inc(sems["gd"].h, 16)
            p.emit("gpsimd", fout)
            ccv = sems["cc"].inc()

            def fag(eng, gdv=gdv):
                eng.wait_ge(sems["gd"].h, gdv)
                eng.collective_compute(
                    "AllGather", ALU.bypass,
                    replica_groups=[list(range(NCORE))],
                    ins=[agx_in.ap().opt()], outs=[agx_out.ap().opt()],
                ).then_inc(sems["cc"].h, 1)
            p.emit("gpsimd", fag)
            return ccv
        return None

    def phase_xgsum():
        for tb in range(2):
            evn = sems["ev"].n
            pev = sems["pe"].inc()

            def fmm(eng, tb=tb, evn=evn):
                eng.wait_ge(sems["ev"].h, evn)
                eng.wait_ge(sems["cld"].h, sems["cld"].n)
                ins = None
                for kc in range(KC):
                    ins = eng.matmul(ps[4][0:GG, 0:512],
                               cbf[:, 128 + (GG - kc):128 + (2 * GG - kc)],
                               big[:, kc * TPC + tb * 512:kc * TPC + (tb + 1) * 512],
                               start=(kc == 0), stop=(kc == KC - 1))
                ins.then_inc(sems["pe"].h, 1)
            p.emit("tensor", fmm)
            evv = sems["ev"].inc()

            def fev(eng, tb=tb, pev=pev):
                eng.wait_ge(sems["pe"].h, pev)
                eng.tensor_copy(sums[0:GG, tb * 512:(tb + 1) * 512],
                                ps[4][0:GG, 0:512]).then_inc(sems["ev"].h, 1)
            p.emit("vector", fev)

    def phase_zs():
        consts_gd = sems["gd"].n
        jobs = [(gz_bytes, sgf_bf, zsg_d, GG, 1376, I),
                (uz_bytes, suf_bf, zsu_d, GG, 512 + 864, I),
                (dz_bytes, sdf_bf, zsd_d, GD, 512, D)]
        jobs[1] = (uz_bytes, suf_bf, zsu_d, GG, 1376, I)
        for (zb, sf, zout, P_, CI, N_) in jobs:
            gv = sems["gd"].n
            v_b = dma("sync", u8a[0:P_, 0:4 * CI], zb.ap(), sems["ldz"],
                      waits=[(sems["gd"], gv)])
            v_s = dma("sync", wbuf[0:P_, 0:N_], sf.ap(), sems["ldz"],
                      waits=[(sems["gd"], consts_gd)])
            exv = sems["ex"].n + 3
            sems["ex"].inc(3)

            def f(eng, P_=P_, CI=CI, N_=N_, v=v_s, base=exv - 3):
                eng.wait_ge(sems["ldz"].h, v)
                ins = unpack_bytes(eng, u8a[0:P_, 5632:5632 + N_],
                                   u8a[0:P_, 0:4 * CI], CI)
                ins.then_inc(sems["ex"].h, 1)
                eng.wait_ge(sems["ex"].h, base + 1)
                eng.tensor_scalar(wbuf[0:P_, 0:N_], wbuf[0:P_, 0:N_],
                                  -1.0, None, ALU.mult).then_inc(sems["ex"].h, 1)
                eng.wait_ge(sems["ex"].h, base + 2)
                eng.tensor_tensor(wbuf[0:P_, 11008:11008 + N_],
                                  u8a[0:P_, 5632:5632 + N_],
                                  wbuf[0:P_, 0:N_],
                                  ALU.mult).then_inc(sems["ex"].h, 1)
            p.emit("vector", f)
            dma("gpsimd", zout.ap(), wbuf[0:P_, 11008:11008 + N_], sems["gd"],
                waits=[(sems["ex"], exv)])

    def dequant_gateup(mat):
        bytes_t = g_bytes if mat == 0 else u_bytes
        s_bf_t = sg_bf if mat == 0 else su_bf
        w_in = wg_in if mat == 0 else wu_in
        w_c = wg_c if mat == 0 else wu_c
        ccvs = []
        ex_zs = sems["ex"].n
        for nb in range(4):
            ch0, chn = NBLK_OFF[nb], NBLK_CH[nb]
            width = chn * 128
            for kcm in range(4):
                stv = sems["st"].n
                dq_pre = sems["dq"].n
                v_b = dma("sync", u8a[:, 0:width // 2],
                          bytes_t[128 * kcm:128 * (kcm + 1),
                                  ch0 * 64:(ch0 + chn) * 64], sems["ld2"],
                          waits=[(sems["dq"], dq_pre), (sems["ex"], ex_zs)])
                v_s = dma("sync", wbuf[:, 22016:22016 + width],
                          bcast_row(s_bf_t, kcm, ch0 * 128, width), sems["ld2"],
                          waits=[(sems["st"], stv)])
                dqv = sems["dq"].n + 2
                sems["dq"].inc(2)

                def f(eng, width=width, v=max(v_b, v_s), stv=stv, base=dqv - 2):
                    eng.wait_ge(sems["ld2"].h, v)
                    eng.wait_ge(sems["st"].h, stv)
                    ins = unpack_bytes(eng, u8a[:, 1408:1408 + width],
                                       u8a[:, 0:width // 2], width // 8)
                    ins.then_inc(sems["dq"].h, 1)
                    eng.wait_ge(sems["dq"].h, base + 1)
                    eng.tensor_tensor(wbuf[:, 24832:24832 + width],
                                      u8a[:, 1408:1408 + width],
                                      wbuf[:, 22016:22016 + width],
                                      ALU.mult).then_inc(sems["dq"].h, 1)
                p.emit("vector", f)
                stv2 = sems["st"].dma()

                def g(eng, w_in=w_in, nb=nb, kcm=kcm, chn=chn, dqv=dqv, width=width):
                    eng.wait_ge(sems["dq"].h, dqv)
                    t = w_in[nb].ap()
                    o = bass.AP(t.tensor, kcm * chn * 128 * 128,
                                [[128, 128], [128 * 128, chn], [1, 128]])
                    src = wbuf[:, 24832:24832 + width].rearrange(
                        "p (c n) -> p c n", n=128)
                    eng.dma_start(out=o, in_=src).then_inc(sems["st"].h, 16)
                p.emit("gpsimd", g)
            stv = sems["st"].n
            ccv = sems["cc"].inc()
            ccvs.append(ccv)

            def ag(eng, w_in=w_in, w_c=w_c, nb=nb, stv=stv):
                eng.wait_ge(sems["st"].h, stv)
                eng.collective_compute(
                    "AllGather", ALU.bypass,
                    replica_groups=[list(range(NCORE))],
                    ins=[w_in[nb].ap().opt()], outs=[w_c[nb].ap().opt()],
                ).then_inc(sems["cc"].h, 1)
            p.emit("gpsimd", ag)
        return ccvs

    def dequant_down():
        for ic in range(ICN):
            stv = sems["st"].n
            dq_pre = sems["dq"].n
            v_b = dma("sync", u8a[:, 0:256],
                      d_bytes[128 * ic:128 * (ic + 1), :], sems["ld2"],
                      waits=[(sems["dq"], dq_pre)])
            v_s = dma("sync", wbuf[:, 22016:22016 + 512],
                      bcast_row(sd_bf, ic, 0, 512), sems["ld2"],
                      waits=[(sems["st"], stv)])
            dqv = sems["dq"].n + 2
            sems["dq"].inc(2)

            def f(eng, v=max(v_b, v_s), stv=stv, base=dqv - 2):
                eng.wait_ge(sems["ld2"].h, v)
                eng.wait_ge(sems["st"].h, stv)
                ins = unpack_bytes(eng, u8a[:, 1408:1408 + 512], u8a[:, 0:256], 64)
                ins.then_inc(sems["dq"].h, 1)
                eng.wait_ge(sems["dq"].h, base + 1)
                eng.tensor_tensor(wbuf[:, 24832:24832 + 512],
                                  u8a[:, 1408:1408 + 512],
                                  wbuf[:, 22016:22016 + 512],
                                  ALU.mult).then_inc(sems["dq"].h, 1)
            p.emit("vector", f)
            sems["st"].dma()

            def g(eng, ic=ic, dqv=dqv):
                eng.wait_ge(sems["dq"].h, dqv)
                eng.dma_start(out=wd_in[ic, :, :],
                              in_=wbuf[:, 24832:24832 + 512]).then_inc(sems["st"].h, 16)
            p.emit("gpsimd", g)
        stv = sems["st"].n
        ccv = sems["cc"].inc()

        def ag(eng, stv=stv):
            eng.wait_ge(sems["st"].h, stv)
            eng.collective_compute(
                "AllGather", ALU.bypass,
                replica_groups=[list(range(NCORE))],
                ins=[wd_in.ap().opt()], outs=[wd_c.ap().opt()],
            ).then_inc(sems["cc"].h, 1)
        p.emit("gpsimd", ag)
        return ccv

    def emit_ffn(x_src, x_dst, ccv_gate, ccv_up, ccv_down, final, gate_only=False, entry_ev=0, entry_ld=()):
        silu_v = []
        fh_v = []
        pe_v = []
        zld = 0
        gd_zs = sems["gd"].n
        for j in range(ICN):
            nb = 0
            while j >= NBLK_OFF[nb] + NBLK_CH[nb]:
                nb += 1
            jl = j - NBLK_OFF[nb]
            chn = NBLK_CH[nb]
            slot = j % 2
            goff = slot * 4096
            uoff = 8192 + slot * 4096
            hoff = 16384 + slot * 1024
            if jl == 0:
                w = chn * 128
                pw = pe_v[-1] if pe_v else 0
                dma("sync", zs32[0:GG, 0:w],
                    zsg_d[:, NBLK_OFF[nb] * 128:NBLK_OFF[nb] * 128 + w],
                    sems["ldz"], waits=[(sems["pe"], pw)] if pw else [])
                dma("sync", zs32[0:GG, 2816:2816 + w],
                    zsu_d[:, NBLK_OFF[nb] * 128:NBLK_OFF[nb] * 128 + w], sems["ldz"])
                zld = sems["ldz"].n

            pefree = pe_v[-2] if len(pe_v) >= 2 else 0
            ccneed = max(ccv_gate[nb], ccv_up[nb])

            def wtile_ap(cache_nb, jl, chn):
                t = cache_nb.ap()
                return bass.AP(t.tensor, jl * 128 * 128,
                               [[128, 128], [chn * 128 * 128, 32], [1, 128]])

            wsem = sems["wld%d" % slot]

            def fload(eng, nb=nb, jl=jl, chn=chn, goff=goff, uoff=uoff,
                      pefree=pefree, ccneed=ccneed, wsem=wsem, j=j, gd_zs=gd_zs,
                      entry_ld=tuple(entry_ld)):
                if j < 2:
                    eng.wait_ge(sems["gd"].h, gd_zs)
                    for (s2, v2) in entry_ld:
                        eng.wait_ge(s2.h, v2)
                eng.wait_ge(sems["cc"].h, ccneed)
                if pefree:
                    eng.wait_ge(sems["pe"].h, pefree)
                dstg = wbuf[:, goff:goff + 4096].rearrange("p (c n) -> p c n", n=128)
                eng.dma_start(out=dstg, in_=wtile_ap(wg_c[nb], jl, chn)).then_inc(wsem.h, 16)
                dstu = wbuf[:, uoff:uoff + 4096].rearrange("p (c n) -> p c n", n=128)
                eng.dma_start(out=dstu, in_=wtile_ap(wu_c[nb], jl, chn)).then_inc(wsem.h, 16)
            wsem.inc(32)
            p.emit("sync", fload)
            ldv = wsem.n

            sfree = silu_v[-1] if silu_v else 0
            vfree = fh_v[-1] if fh_v else 0
            pev = sems["pe"].inc()
            pe_v.append(pev)

            def fmm(eng, jl=jl, goff=goff, uoff=uoff, ldv=ldv, zld=zld,
                    sfree=sfree, vfree=vfree, wsem=wsem, entry_ev=entry_ev):
                eng.wait_ge(wsem.h, ldv)
                eng.wait_ge(sems["ldz"].h, zld)
                if entry_ev and not sfree:
                    eng.wait_ge(sems["ev"].h, entry_ev)
                if sfree:
                    eng.wait_ge(sems["sc"].h, sfree)
                if vfree:
                    eng.wait_ge(sems["ve"].h, vfree)
                ins = None
                for mi, woff in ((0, goff), (1, uoff)):
                    for kc in range(KC):
                        for tb in range(2):
                            ins = eng.matmul(ps[mi * 2 + tb][:, 0:512],
                                       wbuf[:, woff + kc * 128:woff + (kc + 1) * 128],
                                       big[:, kc * TPC + tb * 512:kc * TPC + (tb + 1) * 512],
                                       start=(kc == 0),
                                       stop=(kc == KC - 1 and "zs" not in _FFN_PARTS))
                    if "zs" in _FFN_PARTS:
                        zoff = mi * 2816 + jl * 128
                        for tb in range(2):
                            ins = eng.matmul(ps[mi * 2 + tb][:, 0:512],
                                       zs32[0:GG, zoff:zoff + 128],
                                       sums[0:GG, tb * 512:(tb + 1) * 512],
                                       start=False, stop=True)
                ins.then_inc(sems["pe"].h, 1)
            p.emit("tensor", fmm)

            scv = sems["sc"].inc()
            silu_v.append(scv)

            def fsilu(eng, pev=pev):
                eng.wait_ge(sems["pe"].h, pev)
                ins = None
                for tb in range(2):
                    if "silu" in _FFN_PARTS:
                        ins = eng.activation(wbuf[:, 18432 + tb * 512:18432 + (tb + 1) * 512],
                                             ps[tb][:, 0:512], AF.Silu)
                    else:
                        ins = eng.copy(wbuf[:, 18432 + tb * 512:18432 + (tb + 1) * 512],
                                       ps[tb][:, 0:512])
                ins.then_inc(sems["sc"].h, 1)
            p.emit("scalar", fsilu)

            hs_pre = sems["hsum"].n - 1 if j >= 2 else 0
            hsem = sems["phs%d" % slot]
            ph_pre = hsem.n if j >= 2 else 0
            vev = sems["ve"].inc()
            fh_v.append(vev)

            def fh(eng, scv=scv, hoff=hoff, hs_pre=hs_pre, ph_pre=ph_pre, hsem=hsem):
                eng.wait_ge(sems["sc"].h, scv)
                if hs_pre > 0:
                    eng.wait_ge(sems["hsum"].h, hs_pre)
                if ph_pre > 0:
                    eng.wait_ge(hsem.h, ph_pre)
                ins = None
                for tb in range(2):
                    if "silu" in _FFN_PARTS:
                        ins = eng.tensor_tensor(wbuf[:, hoff + tb * 512:hoff + (tb + 1) * 512],
                                          wbuf[:, 18432 + tb * 512:18432 + (tb + 1) * 512],
                                          ps[2 + tb][:, 0:512], ALU.mult)
                    else:
                        ins = eng.tensor_copy(wbuf[:, hoff + tb * 512:hoff + (tb + 1) * 512],
                                        ps[2 + tb][:, 0:512])
                ins.then_inc(sems["ve"].h, 1)
            p.emit("vector", fh)

            hsv = sems["hsum"].inc()

            def fhg(eng, j=j, hoff=hoff, vev=vev):
                eng.wait_ge(sems["ve"].h, vev)
                if "hg" in _FFN_PARTS:
                    ins = None
                    for tb in range(2):
                        ins = eng.matmul(ps[4 + tb][0:GD, 0:512],
                                   cbf[0:128, 192 + (GD - j):192 + (GD - j) + GD],
                                   wbuf[:, hoff + tb * 512:hoff + (tb + 1) * 512],
                                   start=(j == 0), stop=(j == ICN - 1))
                    ins.then_inc(sems["hsum"].h, 1)
                else:
                    eng.nop().then_inc(sems["hsum"].h, 1)
            p.emit("tensor", fhg)

            hsem.dma()

            def fst(eng, j=j, hoff=hoff, vev=vev, hsem=hsem):
                eng.wait_ge(sems["ve"].h, vev)
                eng.dma_start(out=h_cache[j, :, :],
                              in_=wbuf[:, hoff:hoff + 1024]).then_inc(hsem.h, 16)
            p.emit("gpsimd", fst)

        hsv = sems["hsum"].n
        hgev = sems["ev"].inc()

        def fhge(eng, hsv=hsv):
            eng.wait_ge(sems["hsum"].h, hsv)
            if "hg" in _FFN_PARTS:
                ins = None
                for tb in range(2):
                    ins = eng.tensor_copy(sums[0:GD, 1024 + tb * 512:1024 + (tb + 1) * 512],
                                    ps[4 + tb][0:GD, 0:512])
                ins.then_inc(sems["ev"].h, 1)
            else:
                eng.engine_nop().then_inc(sems["ev"].h, 1)
        p.emit("vector", fhge)

        if gate_only:
            return
        hv0, hv1 = sems["phs0"].n, sems["phs1"].n
        gate_sc = silu_v[-1]
        gate_ve = fh_v[-1]
        stt_ev = {}
        stt_st = {}
        prev_fzs = 0
        for tg in range(2):
            for dp in range(4):
                pe_dn = [prev_fzs, prev_fzs] if prev_fzs else []
                for ic in range(ICN):
                    slot = ic % 2
                    wdoff = slot * 1024
                    hoff2 = 2048 + slot * 512
                    pefree = pe_dn[-2] if len(pe_dn) >= 2 else 0
                    dsem = sems["dld%d" % slot]
                    waits = [(sems["phs0"], hv0), (sems["phs1"], hv1),
                             (sems["cc"], ccv_down)]
                    if pefree:
                        waits.append((sems["pe"], pefree))
                    dma("sync", wbuf[:, wdoff:wdoff + 512],
                        wd_c[2 * dp, ic, :, :], dsem, waits=waits)
                    dma("sync", wbuf[:, wdoff + 512:wdoff + 1024],
                        wd_c[2 * dp + 1, ic, :, :], dsem)
                    v_h = dma("sync", wbuf[:, hoff2:hoff2 + 512],
                              h_cache[ic, :, tg * 512:(tg + 1) * 512], dsem)
                    pev = sems["pe"].inc()
                    pe_dn.append(pev)
                    evfree = sems["ev"].n if ic == 0 else 0

                    def fmm(eng, ic=ic, wdoff=wdoff, hoff2=hoff2, v_h=v_h,
                            evfree=evfree, gate_sc=gate_sc, gate_ve=gate_ve, dsem=dsem):
                        eng.wait_ge(dsem.h, v_h)
                        if ic == 0:
                            eng.wait_ge(sems["ev"].h, evfree)
                            eng.wait_ge(sems["sc"].h, gate_sc)
                            eng.wait_ge(sems["ve"].h, gate_ve)
                        ins = None
                        for tc in range(4):
                            for db2 in range(2):
                                ins = eng.matmul(ps[tc * 2 + db2][:, 0:512],
                                           wbuf[:, hoff2 + tc * 128:hoff2 + (tc + 1) * 128],
                                           wbuf[:, wdoff + db2 * 512:wdoff + (db2 + 1) * 512],
                                           start=(ic == 0), stop=False)
                        ins.then_inc(sems["pe"].h, 1)
                    p.emit("tensor", fmm)

                dma("sync", zs86[0:GD, 0:512],
                    zsd_d[:, (2 * dp) * 512:(2 * dp + 1) * 512], sems["ldz"],
                    waits=[(sems["pe"], prev_fzs)] if prev_fzs else [])
                dma("sync", zs86[0:GD, 512:1024],
                    zsd_d[:, (2 * dp + 1) * 512:(2 * dp + 2) * 512], sems["ldz"])
                zsv = sems["ldz"].n
                pev = sems["pe"].inc()

                def fzs(eng, tg=tg, zsv=zsv, hgev=hgev):
                    eng.wait_ge(sems["ldz"].h, zsv)
                    eng.wait_ge(sems["ev"].h, hgev)
                    ins = None
                    for tc in range(4):
                        toff = 1024 + tg * 512 + tc * 128
                        for db2 in range(2):
                            ins = eng.matmul(ps[tc * 2 + db2][:, 0:512],
                                       sums[0:GD, toff:toff + 128],
                                       zs86[0:GD, db2 * 512:(db2 + 1) * 512],
                                       start=False, stop=True)
                    ins.then_inc(sems["pe"].h, 1)
                p.emit("tensor", fzs)
                prev_fzs = pev

                for tc in range(4):
                    for db2 in range(2):
                        idx = tc * 2 + db2
                        t0 = tg * 4 + tc
                        dcol = dp * 1024 + db2 * 512
                        so_in = 8192 + idx * 512
                        so_out = 12288 + idx * 512
                        evp = stt_ev.get(idx, 0)
                        xs = sems[f"dxl{idx}"]
                        v_x = dma("sync", wbuf_f32[:, so_in:so_in + 512],
                                  x_src[t0 * 128:(t0 + 1) * 128, dcol:dcol + 512],
                                  xs,
                                  waits=[(sems["ev"], evp)] if evp else [])
                        evv = sems["ev"].inc()
                        stt_ev[idx] = evv

                        def fev(eng, idx=idx, so_in=so_in, so_out=so_out,
                                v_x=v_x, pev=pev, stp2=stt_st.get(idx, 0), xs=xs):
                            eng.wait_ge(sems["pe"].h, pev)
                            eng.wait_ge(xs.h, v_x)
                            if stp2:
                                eng.wait_ge(sems[f"ost{idx}"].h, stp2)
                            eng.scalar_tensor_tensor(
                                wbuf_f32[:, so_out:so_out + 512],
                                ps[idx][:, 0:512], 0.5,
                                wbuf_f32[:, so_in:so_in + 512],
                                ALU.mult, ALU.add).then_inc(sems["ev"].h, 1)
                        p.emit("vector", fev)
                        stv = sems[f"ost{idx}"].dma()
                        stt_st[idx] = stv

                        def fso(eng, t0=t0, dcol=dcol, so_out=so_out, evv=evv, idx=idx):
                            eng.wait_ge(sems["ev"].h, evv)
                            dst = (y_out if final else x_dst)
                            eng.dma_start(
                                out=dst[t0 * 128:(t0 + 1) * 128, dcol:dcol + 512],
                                in_=wbuf_f32[:, so_out:so_out + 512]
                            ).then_inc(sems[f"ost{idx}"].h, 16)
                        p.emit("gpsimd", fso)


    def emit_attention(x_src, x_dst, ccx):
        stt_ev2 = {}
        stt_st2 = {}
        clip_ve = {}
        sq_sc = {}
        """x_src: residual after FFN1 (x_parkA). Produces x2 in x_dst.
        Needs agx AG done (ccx). Uses big for x_hat-rank blocks, wbuf carves."""
        qsem = sems["qs"]
        pe_qkv = 0
        st_q = 0
        for r in range(NCORE):
            bw = pe_qkv
            v_x = dma("sync", big[:, 0:KC * TPC],
                      bass.AP(agx_out.ap().tensor, r * D * TPC,
                              [[TPC, 128], [128 * TPC, KC], [1, TPC]]),
                      sems["xld0"] if r % 2 == 0 else sems["xld1"],
                      waits=[(sems["pe"], bw)] if bw else [(sems["cc"], ccx)])
            xsem = sems["xld0"] if r % 2 == 0 else sems["xld1"]
            for eb in range(3):
                wv = dma("sync", wbuf[:, 16384:32768].rearrange("p (c n) -> p c n", n=512),
                         bass.AP(wqkv_bf.ap().tensor, eb * 512,
                                 [[1536 * 128, 128], [1536 * 128, 0], [1, 0]]) if False else
                         bass.AP(wqkv_bf.ap().tensor, eb * 512,
                                 [[1536, 128], [1536 * 128, 32], [1, 512]]),
                         sems["wld0"],
                         waits=[(sems["pe"], pe_qkv)] if pe_qkv else [])
                pev = sems["pe"].inc()

                scpre_q = sems["sc"].n
                vepre_q = sems["ve"].n

                def fq(eng, r=r, eb=eb, v_x=v_x, wv=wv, xsem=xsem,
                       scpre_q=scpre_q, vepre_q=vepre_q):
                    eng.wait_ge(xsem.h, v_x)
                    eng.wait_ge(sems["wld0"].h, wv)
                    eng.wait_ge(sems["sc"].h, scpre_q)
                    eng.wait_ge(sems["ve"].h, vepre_q)
                    ins = None
                    for kc in range(KC):
                        for tc in range(8):
                            ins = eng.matmul(
                                ps[tc][:, 0:512],
                                big[:, kc * TPC + tc * 128:kc * TPC + (tc + 1) * 128],
                                wbuf[:, 16384 + kc * 512:16384 + (kc + 1) * 512],
                                start=(kc == 0), stop=(kc == KC - 1))
                    ins.then_inc(sems["pe"].h, 1)
                p.emit("tensor", fq)
                pe_qkv = pev

                for tc in range(8):
                    scv = sems["sc"].inc()
                    cvp = clip_ve.get(tc, 0)

                    def fcl(eng, tc=tc, pev=pev, cvp=cvp):
                        eng.wait_ge(sems["pe"].h, pev)
                        if cvp:
                            eng.wait_ge(sems["ve"].h, cvp)
                        eng.activation(wbuf[:, 12288 + tc * 512:12288 + (tc + 1) * 512],
                                       ps[tc][:, 0:512],
                                       AF.Identity).then_inc(sems["sc"].h, 1)
                    p.emit("scalar", fcl)
                    vev = sems["ve"].inc()
                    clip_ve[tc] = vev
                    stq_pre = sq_sc.get(tc, 0)

                    def fcl2(eng, tc=tc, scv=scv, stq_pre=stq_pre):
                        eng.wait_ge(sems["sc"].h, scv)
                        if stq_pre:
                            eng.wait_ge(sems[f"ost{tc}"].h, stq_pre)
                        eng.tensor_scalar(wbuf[:, 8192 + tc * 512:8192 + (tc + 1) * 512],
                                          wbuf[:, 12288 + tc * 512:12288 + (tc + 1) * 512],
                                          CLIP, -CLIP, ALU.min, ALU.max).then_inc(sems["ve"].h, 1)
                    p.emit("vector", fcl2)
                    if eb < 2:
                        vev2 = sems["ve"].inc()

                        def fs1(eng, tc=tc, r=r, eb=eb, vev=vev):
                            eng.wait_ge(sems["ve"].h, vev)
                            col = (r * 8 + tc) * 4 + eb * 2
                            eng.reduce_sum(stats_sb[:, col:col + 1],
                                           wbuf[:, 8192 + tc * 512:8192 + (tc + 1) * 512],
                                           axis=AX).then_inc(sems["ve"].h, 1)
                        p.emit("vector", fs1)
                        scv2 = sems["sc"].inc()

                        def fs2(eng, tc=tc, r=r, eb=eb, vev=vev):
                            eng.wait_ge(sems["ve"].h, vev)
                            col = (r * 8 + tc) * 4 + eb * 2 + 1
                            eng.activation(wbuf[:, 12288 + tc * 512:12288 + (tc + 1) * 512],
                                           wbuf[:, 8192 + tc * 512:8192 + (tc + 1) * 512],
                                           AF.Square,
                                           accum_out=stats_sb[:, col:col + 1]).then_inc(sems["sc"].h, 1)
                        p.emit("scalar", fs2)
                    stq = sems[f"ost{tc}"].dma()
                    sq_sc[tc] = stq
                    vevF = sems["ve"].n
                    scvF = sems["sc"].n

                    def fpk(eng, tc=tc, r=r, eb=eb, vevF=vevF, scvF=scvF):
                        eng.wait_ge(sems["ve"].h, vevF)
                        eng.wait_ge(sems["sc"].h, scvF)
                        eng.dma_start(
                            out=qkv_park[(r * 8 + tc) * 128:(r * 8 + tc + 1) * 128,
                                         eb * 512:(eb + 1) * 512],
                            in_=wbuf[:, 8192 + tc * 512:8192 + (tc + 1) * 512]
                        ).then_inc(sems[f"ost{tc}"].h, 16)
                    p.emit("gpsimd", fpk)
        stv = sems["st"].n
        gdv = sems["gd"].dma()

        ve_st = sems["ve"].n
        sc_st = sems["sc"].n
        ostv = [sems[f"ost{i}"].n for i in range(8)]

        def fstw(eng, stv=stv, ve_st=ve_st, sc_st=sc_st, ostv=tuple(ostv)):
            eng.wait_ge(sems["st"].h, stv)
            for i in range(8):
                eng.wait_ge(sems[f"ost{i}"].h, ostv[i])
            eng.wait_ge(sems["ve"].h, ve_st)
            eng.wait_ge(sems["sc"].h, sc_st)
            eng.dma_start(out=st_in.ap(),
                          in_=stats_sb[:, 0:256]).then_inc(sems["gd"].h, 16)
        p.emit("gpsimd", fstw)
        ccs = sems["cc"].inc()

        def fsar(eng, gdv=gdv):
            eng.wait_ge(sems["gd"].h, gdv)
            eng.collective_compute(
                "AllReduce", ALU.add,
                replica_groups=[list(range(NCORE))],
                ins=[st_in.ap().opt()], outs=[st_out.ap().opt()],
            ).then_inc(sems["cc"].h, 1)
        p.emit("gpsimd", fsar)
        v_st = dma("sync", stats_sb[:, 0:256], st_out.ap(), sems["ldz"],
                   waits=[(sems["cc"], ccs)])

        bank_ev2 = {4: 0, 5: 0, 6: 0, 7: 0}
        pe_tr = 0
        prev_fstq = 0
        for rc in range(64):
            for eb in range(2):
                slot = (rc * 2 + eb) % 2
                qoff = slot * 512
                xsem = sems["xld0"] if slot == 0 else sems["xld1"]
                scpre = sems["sc"].n

                def wguard(eng, scpre=scpre):
                    eng.wait_ge(sems["sc"].h, scpre)
                v_q = dma("sync", wbuf[:, qoff:qoff + 512],
                          qkv_park[rc * 128:(rc + 1) * 128, eb * 512:(eb + 1) * 512],
                          xsem, waits=[(sems["sc"], scpre), (sems["ldz"], v_st)] +
                          [(sems[f"ost{i}"], ostv[i]) for i in range(8)])
                col = rc * 4 + eb * 2
                vvq = sems["ve"].n + 3
                sems["ve"].inc(3)
                scq = sems["sc"].inc()

                def fstat(eng, col=col, base=vvq - 3, sb2=(slot * 16)):
                    eng.wait_ge(sems["ldz"].h, v_st)
                    eng.tensor_scalar(smalls[:, sb2 + 0:sb2 + 1], stats_sb[:, col:col + 1],
                                      1.0 / D, None, ALU.mult)
                    eng.tensor_scalar(smalls[:, sb2 + 1:sb2 + 2], stats_sb[:, col + 1:col + 2],
                                      1.0 / D, None, ALU.mult).then_inc(sems["ve"].h, 1)
                    eng.wait_ge(sems["ve"].h, base + 1)
                    eng.scalar_tensor_tensor(
                        smalls[:, sb2 + 2:sb2 + 3], smalls[:, sb2 + 0:sb2 + 1],
                        smalls[:, sb2 + 0:sb2 + 1], smalls[:, sb2 + 1:sb2 + 2],
                        ALU.mult, ALU.subtract).then_inc(sems["ve"].h, 1)
                    eng.wait_ge(sems["ve"].h, base + 2)
                    eng.tensor_scalar(smalls[:, sb2 + 2:sb2 + 3], smalls[:, sb2 + 2:sb2 + 3],
                                      -1.0, EPS, ALU.mult, ALU.add).then_inc(sems["ve"].h, 1)
                p.emit("vector", fstat)

                def fsqr(eng, base=vvq, sb2=(slot * 16)):
                    eng.wait_ge(sems["ve"].h, base)
                    eng.activation(smalls[:, sb2 + 3:sb2 + 4], smalls[:, sb2 + 2:sb2 + 3],
                                   AF.Sqrt).then_inc(sems["sc"].h, 1)
                p.emit("scalar", fsqr)
                vvq2 = sems["ve"].n + 3
                sems["ve"].inc(3)

                def frcp(eng, scq=scq, base=vvq2 - 3, sb2=(slot * 16), eb=eb):
                    eng.wait_ge(sems["sc"].h, scq)
                    eng.reciprocal(smalls[:, sb2 + 4:sb2 + 5],
                                   smalls[:, sb2 + 3:sb2 + 4]).then_inc(sems["ve"].h, 1)
                    eng.wait_ge(sems["ve"].h, base + 1)
                    eng.tensor_scalar(smalls[:, sb2 + 4:sb2 + 5], smalls[:, sb2 + 4:sb2 + 5],
                                      RS128 if eb == 0 else 1.0, None,
                                      ALU.mult).then_inc(sems["ve"].h, 1)
                    eng.wait_ge(sems["ve"].h, base + 2)
                    eng.tensor_scalar(smalls[:, sb2 + 5:sb2 + 6], smalls[:, sb2 + 4:sb2 + 5],
                                      smalls[:, sb2 + 0:sb2 + 1], -1.0,
                                      ALU.mult, ALU.mult).then_inc(sems["ve"].h, 1)
                p.emit("vector", frcp)
                vvq3 = vvq2
                scap = sems["sc"].inc()

                def fapq(eng, qoff=qoff, vvq3=vvq3, sb2=(slot * 16), v_q=v_q, xsem=xsem):
                    eng.wait_ge(xsem.h, v_q)
                    eng.wait_ge(sems["ve"].h, vvq3)
                    eng.activation(wbuf_f32[:, 3072 + (qoff // 512) * 512:3584 + (qoff // 512) * 512]
                                   if False else wbuf_f32[:, 3072:3584],
                                   wbuf[:, qoff:qoff + 512], AF.Identity,
                                   bias=smalls[:, sb2 + 5:sb2 + 6],
                                   scale=smalls[:, sb2 + 4:sb2 + 5]).then_inc(sems["sc"].h, 1)
                p.emit("scalar", fapq)
                vvq4 = sems["ve"].n + 2
                sems["ve"].inc(2)

                def fwb(eng, scap=scap, eb=eb, base=vvq4 - 2):
                    eng.wait_ge(sems["sc"].h, scap)
                    eng.tensor_tensor(wbuf_f32[:, 3584:4096], wbuf_f32[:, 3072:3584],
                                      bf16a[:, eb * 1024:eb * 1024 + 512],
                                      ALU.mult).then_inc(sems["ve"].h, 1)
                    eng.wait_ge(sems["ve"].h, base + 1)
                    eng.tensor_tensor(wbuf[:, 1024:1536], wbuf_f32[:, 3584:4096],
                                      bf16a[:, eb * 1024 + 512:eb * 1024 + 1024],
                                      ALU.add).then_inc(sems["ve"].h, 1)
                p.emit("vector", fwb)
                for fc in range(4):
                    bank = 4 + fc
                    bfree = bank_ev2[bank]
                    pev = sems["pe"].inc()

                    def ftr2(eng, fc=fc, bank=bank, bfree=bfree, base=vvq4):
                        eng.wait_ge(sems["ve"].h, base)
                        if bfree:
                            eng.wait_ge(sems["ev"].h, bfree)
                        eng.transpose(ps_bf[bank][:, 0:128],
                                      wbuf[:, 1024 + fc * 128:1024 + (fc + 1) * 128],
                                      cbf[:, 0:128]).then_inc(sems["pe"].h, 1)
                    p.emit("tensor", ftr2)
                    evv = sems["ev"].inc()
                    bank_ev2[bank] = evv

                    def fev2(eng, fc=fc, bank=bank, pev=pev, pfq=prev_fstq):
                        eng.wait_ge(sems["pe"].h, pev)
                        if pfq:
                            eng.wait_ge(sems["st"].h, pfq)
                        eng.tensor_copy(wbuf[:, 1536 + fc * 128:1536 + (fc + 1) * 128],
                                        ps_bf[bank][:, 0:128]).then_inc(sems["ev"].h, 1)
                    p.emit("vector", fev2)
                evq = sems["ev"].n
                prev_fstq = sems["st"].dma()

                def fstq(eng, rc=rc, eb=eb, evq=evq):
                    eng.wait_ge(sems["ev"].h, evq)
                    dstt = qhat_d if eb == 0 else khat_d
                    o = bass.AP(dstt.ap().tensor, rc * 128,
                                [[TOK, 128], [128 * TOK, 4], [1, 128]])
                    eng.dma_start(out=o,
                                  in_=wbuf[:, 1536:2048].rearrange("p (c n) -> p c n", n=128)
                                  ).then_inc(sems["st"].h, 16)
                p.emit("gpsimd", fstq)

        stq_all = sems["st"].n
        pe_attn = sems["pe"].n
        prev_fos_st = 0
        for hl in range(HPC):
            for b in range(B):
                kv = dma("sync", wbuf[:, 2048:4096],
                         bass.AP(khat_d.ap().tensor, hl * 128 * TOK + b * S,
                                 [[TOK, 128], [1, S]]),
                         sems["dld0"], waits=[(sems["st"], stq_all),
                                              (sems["pe"], pe_attn)] +
                         [(sems[f"ost{i}"], ostv[i]) for i in range(8)])
                vv_ = dma("sync", wbuf[:, 4096:6144],
                          bass.AP(qkv_park.ap().tensor,
                                  (b * S) * 1536 + 1024 + hl * 128,
                                  [[1536, 128], [1536 * 128, 16], [1, 128]]),
                          sems["dld0"])
                prev_fo_ve = 0
                for qt in range(16):
                    nkb = (qt * 128 + 128 + 511) // 512
                    v_q = dma("sync", wbuf[:, 6144:6272],
                              bass.AP(qhat_d.ap().tensor, hl * 128 * TOK + b * S + qt * 128,
                                      [[TOK, 128], [1, 128]]),
                              sems["dld1"], waits=[(sems["sc"], sems["sc"].n)])
                    pevs = []
                    for kb in range(nkb):
                        w = min(512, qt * 128 + 128 - kb * 512)
                        bank = kb % 2
                        evfree = sems["ev"].n
                        pev = sems["pe"].inc()

                        vepre_k = sems["ve"].n

                        def fqk(eng, kb=kb, w=w, bank=bank, v_q=v_q, kv=vv_,
                                evfree=evfree, vepre_k=vepre_k):
                            eng.wait_ge(sems["dld0"].h, kv)
                            eng.wait_ge(sems["dld1"].h, v_q)
                            eng.wait_ge(sems["ve"].h, vepre_k)
                            eng.matmul(ps[bank][:, 0:w], wbuf[:, 6144:6272],
                                       wbuf[:, 2048 + kb * 512:2048 + kb * 512 + w],
                                       start=True, stop=True).then_inc(sems["pe"].h, 1)
                        p.emit("tensor", fqk)
                        vev = sems["ve"].inc()

                        def fal(eng, kb=kb, w=w, bank=bank, qt=qt, hl=hl, pev=pev):
                            eng.wait_ge(sems["pe"].h, pev)
                            off = 2048 + kb * 512 - qt * 128
                            eng.scalar_tensor_tensor(
                                wbuf_f32[:, 4096 + kb * 512:4096 + kb * 512 + w],
                                consts[:, off:off + w],
                                consts[:, 2688 + hl:2689 + hl],
                                ps[bank][:, 0:w],
                                ALU.mult, ALU.add).then_inc(sems["ve"].h, 1)
                        p.emit("vector", fal)
                        pevs.append(sems["ve"].n)
                    vvm = sems["ve"].n + 3
                    sems["ve"].inc(3)

                    def fmx(eng, qt=qt, nkb=nkb, base=vvm - 3, sb2=0):
                        eng.wait_ge(sems["ve"].h, base)
                        dcol = 4096 + qt * 128
                        eng.tensor_tensor(wbuf_f32[:, dcol:dcol + 128],
                                          wbuf_f32[:, dcol:dcol + 128],
                                          consts[:, 2560:2688], ALU.add).then_inc(sems["ve"].h, 1)
                        eng.wait_ge(sems["ve"].h, base + 1)
                        eng.reduce_max(smalls[:, 32:33],
                                       wbuf_f32[:, 4096:4096 + qt * 128 + 128],
                                       axis=AX).then_inc(sems["ve"].h, 1)
                        eng.wait_ge(sems["ve"].h, base + 2)
                        eng.tensor_scalar(smalls[:, 33:34], smalls[:, 32:33],
                                          -1.0, None, ALU.mult).then_inc(sems["ve"].h, 1)
                    p.emit("vector", fmx)
                    scv = sems["sc"].inc()

                    def fexp(eng, qt=qt, vvm=vvm):
                        eng.wait_ge(sems["ve"].h, vvm)
                        eng.activation(wbuf[:, 14464:14464 + qt * 128 + 128],
                                       wbuf_f32[:, 4096:4096 + qt * 128 + 128],
                                       AF.Exp, bias=smalls[:, 33:34],
                                       accum_out=smalls[:, 34:35]).then_inc(sems["sc"].h, 1)
                    p.emit("scalar", fexp)
                    vvr = sems["ve"].n + 2
                    sems["ve"].inc(2)

                    def frs(eng, scv=scv, base=vvr - 2):
                        eng.wait_ge(sems["sc"].h, scv)
                        eng.reciprocal(smalls[:, 35:36], smalls[:, 34:35]).then_inc(sems["ve"].h, 1)
                        eng.wait_ge(sems["ve"].h, base + 1)
                        eng.tensor_scalar(wbuf[:, 14336:14464], cbf[:, 0:128],
                                          smalls[:, 35:36], None, ALU.mult).then_inc(sems["ve"].h, 1)
                    p.emit("vector", frs)
                    nkt = qt + 1
                    for ktc in range(nkt):
                        bank = 4 + (ktc % 3)
                        bfree = bank_ev2.get(bank, 0)
                        pev = sems["pe"].inc()

                        def ftp(eng, ktc=ktc, bank=bank, bfree=bfree, vvr=vvr):
                            eng.wait_ge(sems["ve"].h, vvr)
                            if bfree:
                                eng.wait_ge(sems["ev"].h, bfree)
                            eng.matmul(ps[bank][:, 0:128],
                                       wbuf[:, 14464 + ktc * 128:14464 + (ktc + 1) * 128],
                                       wbuf[:, 14336:14464],
                                       start=True, stop=True).then_inc(sems["pe"].h, 1)
                        p.emit("tensor", ftp)
                        evv = sems["ev"].inc()
                        bank_ev2[bank] = evv

                        def fte(eng, ktc=ktc, bank=bank, pev=pev):
                            eng.wait_ge(sems["pe"].h, pev)
                            eng.tensor_copy(wbuf[:, 16512 + ktc * 128:16512 + (ktc + 1) * 128],
                                            ps[bank][:, 0:128]).then_inc(sems["ev"].h, 1)
                        p.emit("vector", fte)
                    evt = sems["ev"].n
                    prev_fo = prev_fo_ve if qt > 0 else 0
                    pev2 = sems["pe"].inc()

                    def fav(eng, qt=qt, nkt=nkt, evt=evt, prev_fo=prev_fo):
                        eng.wait_ge(sems["ev"].h, evt)
                        if prev_fo:
                            eng.wait_ge(sems["ve"].h, prev_fo)
                        ins = None
                        for ktc in range(nkt):
                            ins = eng.matmul(ps[7][:, 0:128],
                                             wbuf[:, 4096 + ktc * 128:4096 + (ktc + 1) * 128],
                                             wbuf[:, 16512 + ktc * 128:16512 + (ktc + 1) * 128],
                                             start=(ktc == 0), stop=(ktc == nkt - 1))
                        ins.then_inc(sems["pe"].h, 1)
                    p.emit("tensor", fav)
                    bank_ev2[7] = 0
                    vevo = sems["ve"].inc()
                    prev_fo_ve = vevo

                    def fo(eng, qt=qt, pev2=pev2, pfos=prev_fos_st):
                        eng.wait_ge(sems["pe"].h, pev2)
                        if qt == 0 and pfos:
                            eng.wait_ge(sems["st"].h, pfos)
                        eng.tensor_copy(wbuf[:, 18560 + qt * 128:18560 + (qt + 1) * 128],
                                        ps[7][:, 0:128]).then_inc(sems["ve"].h, 1)
                    p.emit("vector", fo)
                    bank_ev2[7] = sems["ve"].n
                vevo = sems["ve"].n
                sems["st"].dma()
                prev_fos_st = sems["st"].dma()

                def fos(eng, hl=hl, b=b, vevo=vevo):
                    eng.wait_ge(sems["ve"].h, vevo)
                    for half in range(2):
                        shard = 2 * b + half
                        eng.dma_start(
                            out=bass.AP(o_in.ap().tensor,
                                        shard * 512 * TPC + hl * 128 * TPC,
                                        [[TPC, 128], [1, TPC]]),
                            in_=wbuf[:, 18560 + half * 1024:18560 + (half + 1) * 1024]
                        ).then_inc(sems["st"].h, 16)
                p.emit("gpsimd", fos)
                pe_attn = sems["pe"].n
        sto = sems["st"].n
        cco = sems["cc"].inc()

        def fago(eng, sto=sto):
            eng.wait_ge(sems["st"].h, sto)
            from concourse.aot_env import allow_ici_all_to_all
            with allow_ici_all_to_all():
                eng.collective_compute(
                    "AllToAll", ALU.bypass,
                    replica_groups=[list(range(NCORE))],
                    ins=[o_in.ap().opt()], outs=[o_out.ap().opt()],
                ).then_inc(sems["cc"].h, 1)
        p.emit("gpsimd", fago)

        pe_w = sems["pe"].n
        ldv = None
        for dc in range(KC):
            r2, hl2 = dc // 4, dc % 4
            v_o = dma("sync", big[:, dc * TPC:(dc + 1) * TPC],
                      bass.AP(o_out.ap().tensor,
                              r2 * 512 * TPC + hl2 * 128 * TPC,
                              [[TPC, 128], [1, TPC]]),
                      sems["xld0"],
                      waits=[(sems["cc"], cco), (sems["pe"], pe_w)] if dc == 0 else [])
            ldv = v_o
        for eb in range(8):
            wv = dma("sync", wbuf[:, 16384:16384 + 16384].rearrange("p (c n) -> p c n", n=512),
                     bass.AP(wout_bf.ap().tensor, eb * 512,
                             [[D, 128], [D * 128, 32], [1, 512]]),
                     sems["wld1"], waits=[(sems["pe"], sems["pe"].n)])
            ve_pre_w = sems["ve"].n
            ev_pre_w = sems["ev"].n
            pev = sems["pe"].inc()

            def fw(eng, eb=eb, wv=wv, ldv=ldv, ve_pre_w=ve_pre_w, ev_pre_w=ev_pre_w):
                eng.wait_ge(sems["xld0"].h, ldv)
                eng.wait_ge(sems["wld1"].h, wv)
                eng.wait_ge(sems["ve"].h, ve_pre_w)
                eng.wait_ge(sems["ev"].h, ev_pre_w)
                ins = None
                for dc in range(KC):
                    for tc in range(8):
                        ins = eng.matmul(ps[tc][:, 0:512],
                                         big[:, dc * TPC + tc * 128:dc * TPC + (tc + 1) * 128],
                                         wbuf[:, 16384 + dc * 512:16384 + (dc + 1) * 512],
                                         start=(dc == 0), stop=(dc == KC - 1))
                ins.then_inc(sems["pe"].h, 1)
            p.emit("tensor", fw)
            for tc in range(8):
                v_x1 = dma("sync", wbuf_f32[:, 4096 + tc * 512:4096 + (tc + 1) * 512],
                           x_src[tc * 128:(tc + 1) * 128, eb * 512:(eb + 1) * 512],
                           sems[f"dxl{tc}"],
                           waits=[(sems["ev"], stt_ev2.get(tc, 0))] if stt_ev2.get(tc, 0) else [])
                evv = sems["ev"].inc()
                stt_ev2[tc] = evv

                def fx2(eng, tc=tc, eb=eb, v_x1=v_x1, pev=pev, stp=stt_st2.get(tc, 0)):
                    eng.wait_ge(sems["pe"].h, pev)
                    eng.wait_ge(sems[f"dxl{tc}"].h, v_x1)
                    if stp:
                        eng.wait_ge(sems[f"ost{tc}"].h, stp)
                    eng.tensor_tensor(wbuf_f32[:, 0 + tc * 512:0 + (tc + 1) * 512],
                                      ps[tc][:, 0:512],
                                      wbuf_f32[:, 4096 + tc * 512:4096 + (tc + 1) * 512],
                                      ALU.add).then_inc(sems["ev"].h, 1)
                p.emit("vector", fx2)
                stv = sems[f"ost{tc}"].dma()
                stt_st2[tc] = stv

                def fx2s(eng, tc=tc, eb=eb, evv=evv):
                    eng.wait_ge(sems["ev"].h, evv)
                    eng.dma_start(out=x_dst[tc * 128:(tc + 1) * 128, eb * 512:(eb + 1) * 512],
                                  in_=wbuf_f32[:, 0 + tc * 512:0 + (tc + 1) * 512]
                                  ).then_inc(sems[f"ost{tc}"].h, 16)
                p.emit("gpsimd", fx2s)

    def debug_dump_xhat():
        evv = sems["ev"].n
        vv = sems["ve"].n
        scv = sems["sc"].n
        ccv = sems["cc"].n
        sems["st"].dma()

        def f(eng):
            eng.wait_ge(sems["ev"].h, evv)
            eng.wait_ge(sems["ve"].h, vv)
            eng.wait_ge(sems["sc"].h, scv)
            eng.wait_ge(sems["cc"].h, ccv)
            dst = bass.AP(y_out.ap().tensor, 0, [[KC * TPC, 128], [1, KC * TPC]])
            eng.dma_start(out=dst, in_=big[:, 0:KC * TPC]).then_inc(sems["st"].h, 16)
        p.emit("gpsimd", f)

    phase_consts()
    if _SUB != 11:
        phase_ln(x_in, to_ag=False, upto=(_SUB - 20 if 20 <= _SUB < 30 else 99))
    if _SUB in (0, 1, 3, 6, 7, 8, 9):
        phase_xgsum()
    if _SUB == 1 or _SUB == 11 or _SUB == 13 or 20 <= _SUB < 30:
        debug_dump_xhat()
    else:
        barrier()
        phase_zs()
        ccv_gate = dequant_gateup(0)
        ccv_up = dequant_gateup(1)
        ccv_down = dequant_down()
        if _SUB == 2:
            debug_dump_xhat()
        else:
            emit_ffn(x_in, x_parkA, ccv_gate, ccv_up, ccv_down,
                     final=(_STAGE == 1), gate_only=(_SUB in (3, 6, 7, 8, 9)))
            if _STAGE >= 2 and _SUB == 0:
                entry = [(sems["pe"], sems["pe"].n), (sems["ev"], sems["ev"].n)] + \
                        [(sems[f"ost{i}"], sems[f"ost{i}"].n) for i in range(8)]
                ccx = phase_ln(x_parkA, to_ag=True, entry=entry)
                emit_attention(x_parkA, x_parkB, ccx)
                entry2 = [(sems["pe"], sems["pe"].n), (sems["ev"], sems["ev"].n)] + \
                         [(sems[f"ost{i}"], sems[f"ost{i}"].n) for i in range(8)]
                phase_ln(x_parkB, to_ag=False, entry=entry2)
                phase_xgsum()
                emit_ffn(x_parkB, None, ccv_gate, ccv_up, ccv_down,
                         final=True, entry_ev=sems["ev"].n,
                         entry_ld=[(sems["ve"], sems["ve"].n),
                                   (sems["sc"], sems["sc"].n),
                                   (sems["ev"], sems["ev"].n),
                                   (sems["pe"], sems["pe"].n)])
            if _SUB == 6:
                stv6 = sems["st"].n
                evv6 = sems["ev"].n
                sems["st"].dma()
                sems["st"].dma()
                sems["st"].dma()
                sems["st"].dma()

                def fdz(eng, evv6=evv6):
                    eng.wait_ge(sems["cc"].h, sems["cc"].n)
                    eng.wait_ge(sems["ev"].h, evv6)
                    eng.dma_start(out=y_out[0:32, :], in_=zsg_d[:, 0:4096]).then_inc(sems["st"].h, 16)
                    eng.dma_start(out=y_out[32:64, :], in_=zsu_d[:, 0:4096]).then_inc(sems["st"].h, 16)
                    eng.dma_start(out=y_out[64:150, :], in_=zsd_d[:, 0:4096]).then_inc(sems["st"].h, 16)
                    eng.dma_start(out=y_out[150:182, 0:1024], in_=sums[0:GG, 0:1024]).then_inc(sems["st"].h, 16)
                p.emit("gpsimd", fdz)
            if _SUB == 9:
                ccv9 = sems["cc"].n
                evv9 = sems["ev"].n
                for _ in range(6):
                    sems["st"].dma()

                def fdz9(eng, ccv9=ccv9, evv9=evv9):
                    eng.wait_ge(sems["cc"].h, ccv9)
                    eng.wait_ge(sems["ev"].h, evv9)
                    eng.dma_start(out=y_out[0:32, :], in_=zsg_d[:, 6912:11008]).then_inc(sems["st"].h, 16)
                    eng.dma_start(out=y_out[32:64, :], in_=zsu_d[:, 6912:11008]).then_inc(sems["st"].h, 16)
                    for i, j in enumerate(range(64, 68)):
                        nb = 3 if j >= 65 else 2
                        jl = j - NBLK_OFF[nb]
                        chn = NBLK_CH[nb]
                        t = wu_c[nb].ap()
                        src = bass.AP(t.tensor, jl * 128 * 128,
                                      [[128, 128], [chn * 128 * 128, 32], [1, 128]])
                        dst = y_out[64 + i * 128:64 + (i + 1) * 128, :].rearrange(
                            "p (c n) -> p c n", n=128)
                        eng.dma_start(out=dst, in_=src).then_inc(sems["st"].h, 16)
                p.emit("gpsimd", fdz9)
            if _SUB == 8:
                ccv8 = sems["cc"].n
                for _ in range(8):
                    sems["st"].dma()

                def fdw(eng, ccv8=ccv8):
                    eng.wait_ge(sems["cc"].h, ccv8)
                    for i, j in enumerate(range(60, 68)):
                        nb = 0
                        while j >= NBLK_OFF[nb] + NBLK_CH[nb]:
                            nb += 1
                        jl = j - NBLK_OFF[nb]
                        chn = NBLK_CH[nb]
                        t = wg_c[nb].ap()
                        src = bass.AP(t.tensor, jl * 128 * 128,
                                      [[128, 128], [chn * 128 * 128, 32], [1, 128]])
                        dst = y_out[i * 128:(i + 1) * 128, :].rearrange(
                            "p (c n) -> p c n", n=128)
                        eng.dma_start(out=dst, in_=src).then_inc(sems["st"].h, 16)
                        sems["st"].n += 0
                p.emit("gpsimd", fdw)
            if _SUB == 7:
                phv = sems["phs0"].n
                phv1 = sems["phs1"].n
                sems["st"].dma()

                def fdh2(eng, phv=phv, phv1=phv1):
                    eng.wait_ge(sems["phs0"].h, phv)
                    eng.wait_ge(sems["phs1"].h, phv1)
                    dst = bass.AP(y_out.ap().tensor, 0,
                                  [[KC * TPC, 128], [TPC, KC], [1, TPC]])
                    src = bass.AP(h_cache.ap().tensor, 54 * 128 * TPC,
                                  [[TPC, 128], [128 * TPC, KC], [1, TPC]])
                    eng.dma_start(out=dst, in_=src).then_inc(sems["st"].h, 16)
                p.emit("gpsimd", fdh2)
            if _SUB == 3:
                phv = sems["ph"].n
                sems["st"].dma()

                def fdh(eng, phv=phv):
                    eng.wait_ge(sems["ph"].h, phv)
                    dst = bass.AP(y_out.ap().tensor, 0,
                                  [[KC * TPC, 128], [TPC, KC], [1, TPC]])
                    src = bass.AP(h_cache.ap().tensor, 0,
                                  [[TPC, 128], [128 * TPC, KC], [1, TPC]])
                    eng.dma_start(out=dst, in_=src).then_inc(sems["st"].h, 16)
                p.emit("gpsimd", fdh)

    with nc.Block() as block:
        @block.sync
        def _(eng):
            p.run("sync", eng)

        @block.gpsimd
        def _(eng):
            p.run("gpsimd", eng)
            for nm in ("st", "phs0", "phs1"):
                if sems[nm].n:
                    eng.wait_ge(sems[nm].h, sems[nm].n)
            for i in range(8):
                if sems[f"ost{i}"].n:
                    eng.wait_ge(sems[f"ost{i}"].h, sems[f"ost{i}"].n)

        @block.tensor
        def _(eng):
            p.run("tensor", eng)

        @block.vector
        def _(eng):
            p.run("vector", eng)

        @block.scalar
        def _(eng):
            p.run("scalar", eng)

    global _LAST_SEMS
    _LAST_SEMS = {k: v.n for k, v in sems.items()}
    ctx.close()
    return nc


def _host_inputs(inputs):
    x = np.ascontiguousarray(
        np.asarray(inputs["L0_attn_hidden_states"], np.float32).reshape(TOK, D))
    wqkvT = np.ascontiguousarray(np.asarray(inputs["L0_attn_Wqkv"], np.float32).T)
    woutT = np.ascontiguousarray(np.asarray(inputs["L0_attn_Wout"], np.float32).T)
    g_bytes = np.asarray(inputs["L0_ffn_gate_qweight"], np.int32).view(np.uint8).reshape(D, 5504)
    u_bytes = np.asarray(inputs["L0_ffn_up_qweight"], np.int32).view(np.uint8).reshape(D, 5504)
    d_bytes = np.asarray(inputs["L0_ffn_down_qweight"], np.int32).view(np.uint8).reshape(I, 2048)
    gs = np.ascontiguousarray(np.asarray(inputs["L0_ffn_gate_scales"], np.float32))
    us = np.ascontiguousarray(np.asarray(inputs["L0_ffn_up_scales"], np.float32))
    ds = np.ascontiguousarray(np.asarray(inputs["L0_ffn_down_scales"], np.float32))
    gz = np.asarray(inputs["L0_ffn_gate_qzeros"], np.int32).view(np.uint8).reshape(GG, 5504)
    uz = np.asarray(inputs["L0_ffn_up_qzeros"], np.int32).view(np.uint8).reshape(GG, 5504)
    dz = np.asarray(inputs["L0_ffn_down_qzeros"], np.int32).view(np.uint8).reshape(GD, 2048)
    qw = np.asarray(inputs["L0_attn_q_ln_w"], np.float32)
    qb = np.asarray(inputs["L0_attn_q_ln_b"], np.float32)
    kw = np.asarray(inputs["L0_attn_k_ln_w"], np.float32)
    kb = np.asarray(inputs["L0_attn_k_ln_b"], np.float32)
    slopes = _alibi_slopes()

    ident = np.eye(128, dtype=ml_dtypes.bfloat16)
    stair32 = np.zeros((128, 64), dtype=ml_dtypes.bfloat16)
    stair32[:, GG] = 1
    stair86 = np.zeros((128, 172), dtype=ml_dtypes.bfloat16)
    stair86[:, GD] = 1
    md = np.zeros((128, 128), np.float32)
    md[np.triu_indices(128, 1)] = -1e30
    rm = (np.arange(2560, dtype=np.float32)[None, :] - 2048.0
          - np.arange(128, dtype=np.float32)[:, None])

    maps = []
    for c in range(NCORE):
        maps.append(dict(
            x_in=np.ascontiguousarray(x[c * TPC:(c + 1) * TPC]),
            wqkvT=np.ascontiguousarray(np.concatenate(
                [wqkvT[:, 512 * c:512 * (c + 1)],
                 wqkvT[:, D + 512 * c:D + 512 * (c + 1)],
                 wqkvT[:, 2 * D + 512 * c:2 * D + 512 * (c + 1)]], axis=1)),
            woutT=woutT,
            qln_w=np.ascontiguousarray(qw[512 * c:512 * (c + 1)].reshape(1, 512)),
            qln_b=np.ascontiguousarray(qb[512 * c:512 * (c + 1)].reshape(1, 512)),
            kln_w=np.ascontiguousarray(kw[512 * c:512 * (c + 1)].reshape(1, 512)),
            kln_b=np.ascontiguousarray(kb[512 * c:512 * (c + 1)].reshape(1, 512)),
            slopes_in=np.ascontiguousarray(slopes[4 * c:4 * (c + 1)].reshape(1, HPC)),
            g_bytes=np.ascontiguousarray(g_bytes[512 * c:512 * (c + 1)]),
            u_bytes=np.ascontiguousarray(u_bytes[512 * c:512 * (c + 1)]),
            d_bytes=np.ascontiguousarray(d_bytes[:, 256 * c:256 * (c + 1)]),
            gs_full=gs, us_full=us, ds_full=ds,
            gz_bytes=gz, uz_bytes=uz, dz_bytes=dz,
            gs_mine=np.ascontiguousarray(gs[4 * c:4 * (c + 1)]),
            us_mine=np.ascontiguousarray(us[4 * c:4 * (c + 1)]),
            ds_mine=np.ascontiguousarray(ds[:, 512 * c:512 * (c + 1)]),
            ident_in=ident, stair32_in=stair32, stair86_in=stair86,
            maskdiag_in=md, rmast_in=rm,
        ))
    return maps


def kernel(**inputs):
    maps = _host_inputs(inputs)
    nc = build_program()
    res = run_bass_kernel_spmd(nc, maps, core_ids=list(range(NCORE)))
    out = np.concatenate(
        [np.asarray(res.results[c]["y_out"]) for c in range(NCORE)], axis=0)
    return out.reshape(B, S, D)
